# version 05 (final)
# baseline (speedup 1.0000x reference)
import dataclasses
import numpy as np
import ml_dtypes
import concourse.bass as bass
import concourse.mybir as mybir
from concourse.bass_utils import run_bass_kernel_spmd
from contextlib import ExitStack

F32 = mybir.dt.float32
BF16 = mybir.dt.bfloat16
AF = mybir.ActivationFunctionType
ALU = mybir.AluOpType
AX = mybir.AxisListType

NCORE = 8
T = 1024
D = 4096
KD = 32
INTER = 8192
KI = 64
LAT = 2048
KL = 16
MP = 145
MC = 80
PH = 64
G = 8
CS = 128
NCH = 8
EPS = 1e-5
BIGM = 1.0e9
KB = 1024
WCH = 1024
NRING = 2

SCRATCH_KIND = "Internal"
_DEBUG_TAP = []
_UPTO = 99
_FSTOP = 99


def rep0(ap, nrep):
    return dataclasses.replace(ap, ap=list(ap.ap) + [[0, nrep]])


def mid0(ap, nrep):
    a = list(ap.ap)
    return dataclasses.replace(ap, ap=[a[0], [0, nrep]] + a[1:])


class Ctr:
    def __init__(self, h):
        self.h = h
        self.v = 0


def inc(instr, c, n=1):
    instr.then_inc(c.h, n)
    c.v += n
    return instr


def wat(eng, c, v=None):
    eng.wait_ge(c.h, c.v if v is None else v)


def build_program():
    nc = bass.Bass("TRN2", target_bir_lowering=False)

    def ein(name, shape, dt=F32):
        return nc.dram_tensor(name, list(shape), dt, kind="ExternalInput")

    xT = ein("xT", (D, T))
    wgate = ein("wgate", (128, KD * 8))
    wfc1 = ein("wfc1", (KL, 128, D))
    wexp = ein("wexp", (KL, 128, LAT))
    wfc2 = ein("wfc2", (KD, 128, LAT))
    wup = ein("wup", (KI, 128, D))
    wdown = ein("wdown", (KD, 128, INTER))
    winp = ein("winp", (MP, 128, D))
    woutp = ein("woutp", (KD, 128, INTER))
    convw = ein("convw", (128, MC * 4))
    convb = ein("convb", (128, MC))
    dtbias = ein("dtbias", (128, 1))
    alog = ein("alog", (128, 1))
    dcol = ein("dcol", (128, 1))
    normw = ein("normw", (128, KI))
    gam = ein("gam", (128, KD))
    bet = ein("bet", (128, KD))
    nmask = ein("nmask", (128, 4))
    hmask = ein("hmask", (128, 4))

    out_xT = nc.dram_tensor("out_xT", [D, T], F32, kind="ExternalOutput")

    id_bf = nc.inline_tensor(np.eye(128, dtype=ml_dtypes.bfloat16), "id_bf")
    id_f32 = nc.inline_tensor(np.eye(128, dtype=np.float32), "id_f32")
    msu_np = np.where(np.arange(128)[None, :] < np.arange(128)[:, None],
                      BIGM, 0.0)
    masksu = nc.inline_tensor(msu_np.astype(ml_dtypes.bfloat16), "masksu")

    scr_handles = {}

    def scr(name, shape, dt):
        h = nc.dram_tensor(name, list(shape), dt, kind=SCRATCH_KIND)
        scr_handles[name] = (h, list(shape), dt)
        return h

    proj_s = scr("proj_s", (MP * 128, T), BF16)
    conv_s = scr("conv_s", (MC * 128, T), BF16)
    moe_s = scr("moe_s", (KL * 128, T), BF16)
    fo_s = scr("fo_s", (KD * 128, T), BF16)
    yraw_s = scr("yraw_s", (NCH * 128, INTER), BF16)
    ynorm_s = scr("ynorm_s", (NCH * 128, INTER), BF16)
    x1_s = scr("x1_s", (D, T), F32)
    x2_s = scr("x2_s", (D, T), F32)
    acflat_s = scr("acflat_s", (3 * NCH, 128 * 128), BF16)
    ag1i = nc.dram_tensor("ag1i", [128, MC * 3], BF16)
    ag1o = nc.dram_tensor("ag1o", [4 * 128, MC * 3], BF16)
    ag2i = nc.dram_tensor("ag2i", [128, INTER + 64], F32)
    ag2o = nc.dram_tensor("ag2o", [4 * 128, INTER + 64], F32)

    ctx = ExitStack()

    def sb(name, shape, dt):
        return nc.alloc_sbuf_tensor(name, list(shape), dt)

    c_idb = sb("c_idb", (128, 128), BF16)
    c_idf = sb("c_idf", (128, 128), F32)
    c_msu = sb("c_msu", (128, 128), BF16)
    c_ones1b = sb("c_ones1b", (1, 128), BF16)
    c_ones1f = sb("c_ones1f", (1, 128), F32)
    c_onescb = sb("c_onescb", (128, 1), BF16)
    c_ones3 = sb("c_ones3", (3, 128), BF16)
    c_gatew = sb("c_gatew", (128, KD * 8), BF16)
    c_dtb = sb("c_dtb", (128, 1), F32)
    c_acol = sb("c_acol", (128, 1), F32)
    c_dcolb = sb("c_dcolb", (128, 1), BF16)
    c_normw = sb("c_normw", (128, KI), F32)
    c_gam = sb("c_gam", (128, KD), F32)
    c_bet = sb("c_bet", (128, KD), F32)
    c_nmask = sb("c_nmask", (128, 4), F32)
    c_hmask = sb("c_hmask", (128, 4), F32)
    wring = [sb(f"wring{i}", (128, WCH), BF16) for i in range(NRING)]
    etile = [sb(f"etile{i}", (128, T), BF16) for i in range(2)]
    lt_sb = sb("lt_sb", (128, 64), F32)
    rwcol = sb("rwcol", (128, 8), F32)
    t128a = sb("t128a", (128, 128), F32)
    t128b = sb("t128b", (128, 128), F32)
    t128c = sb("t128c", (128, 128), F32)
    eacm = sb("eacm", (128, 128), BF16)
    ddtm = sb("ddtm", (128, 128), F32)
    dchbc = sb("dchbc", (128, 128), F32)
    pdbc = sb("pdbc", (128, 128), F32)
    col_a = sb("col_a", (128, 1), F32)
    col_b = sb("col_b", (128, 1), F32)
    col_c = sb("col_c", (128, 1), F32)
    onecol = sb("onecol", (128, 1), F32)
    mscol = sb("mscol", (128, 1), F32)
    dtotc = [sb(f"dtotc{i}", (128, 1), F32) for i in range(4)]
    ucol = [sb(f"ucol{i}", (128, 1), F32) for i in range(4)]
    rowbuf = sb("rowbuf", (1, 128), F32)
    c_eps = sb("c_eps", (128, 1), F32)
    c_inv8k = sb("c_inv8k", (128, 1), F32)
    nc.const_aps.register(F32, EPS, c_eps[:, 0:1])
    nc.const_aps.register(F32, 1.0 / INTER, c_inv8k[:, 0:1])

    AR = 192 * KB
    arena = nc.alloc_sbuf_tensor("arena", [128, AR], mybir.dt.uint8)
    ar0 = nc.lookup_mloc(arena).addr
    _ov = {}

    def at(name, off_kb, shape, dt):
        if name not in _ov:
            _ov[name] = nc.alloc_sbuf_tensor_at(
                name, list(shape), dt, offset=ar0 + int(off_kb * KB))
        return _ov[name]

    hreg = at("hreg", 0, (128, KD * T), BF16)
    rwb = at("rwb", 128, (128, T), F32)
    otileD = [at(f"otileD{i}", 32 + 2 * i, (128, T), BF16) for i in range(2)]
    otileI = [at(f"otileI{i}", 64 + 2 * i, (128, T), BF16) for i in range(2)]
    otileC = [at(f"otileC{i}", 132 + 2 * i, (128, T), BF16) for i in range(2)]
    c_convw = at("c_convw", 136, (128, MC * 4), F32)
    c_convb = at("c_convb", 138, (128, MC), F32)
    btile = [at(f"btile{i}", 64 + 2 * i, (128, T), BF16) for i in range(3)]
    sqt = at("sqt", 70, (128, T), BF16)
    xtile = [at(f"xtile{i}", 72 + 4 * i, (128, T), F32) for i in range(3)]
    ftile = [at(f"ftile{i}", 84 + 4 * i, (128, T), F32) for i in range(2)]
    r_mu = at("r_mu", 92, (1, T), F32)
    r_rs = at("r_rs", 96, (1, T), F32)
    b_mu = at("b_mu", 100, (128, T), F32)
    b_rs = at("b_rs", 104, (128, T), F32)
    l_sb = at("l_sb", 108, (8, T), F32)
    derow = at("derow", 112, (1, INTER), BF16)
    latreg = at("latreg", 64, (128, KL * T), BF16)
    moereg = at("moereg", 96, (128, KL * T), BF16)
    upreg = at("upreg", 64, (128, KI * T), BF16)
    moereg2 = at("moereg2", 0, (128, KL * T), BF16)
    dtraw = at("dtraw", 68, (128, T), F32)
    tails = at("tails", 184, (128, MC * 3), BF16)
    tailp = at("tailp", 186, (128, MC * 3), BF16)
    dtp = at("dtp", 90, (128, T), F32)
    acum = at("acum", 94, (128, T), F32)
    nac_hi = at("nac_hi", 98, (128, T), BF16)
    nac_lo = at("nac_lo", 100, (128, T), BF16)
    nac_lo2 = at("nac_lo2", 102, (128, T), BF16)
    dtT = at("dtT", 104, (128, T), F32)
    zbuf = at("zbuf", 108, (128, T), F32)
    adtb = at("adtb", 112, (128, T), F32)
    tmpb16 = at("tmpb16", 116, (128, T), BF16)
    conv_in = [at(f"conv_in{i}", 118 + 2.5 * i, (128, 1027), BF16)
               for i in range(2)]
    conv_ac = [at(f"conv_ac{i}", 124 + 4 * i, (128, T), F32) for i in range(2)]
    hbc = at("hbc", 0, (128, MC * 128), BF16)
    x_tm = at("x_tm", 20, (128, INTER), BF16)
    y_sb = at("y_sb", 36, (128, INTER), BF16)
    mtsb = at("mtsb", 52, (128, 16 * 128), BF16)
    expd = at("expd", 56, (128, 16 * 128), BF16)
    easb = at("easb", 60, (128, 16 * PH), F32)
    gtm = at("gtm", 64, (128, G * 128), BF16)
    xdd = at("xdd", 66, (128, 16 * PH), BF16)
    xdtile = at("xdtile", 68, (128, 16 * PH), BF16)
    sbfb = at("sbfb", 72, (128, 16 * PH), BF16)
    b_tm = at("b_tm", 74, (128, G * 128), BF16)
    ssd_t1 = at("ssd_t1", 76, (128, 16 * PH), F32)
    acstack = at("acstack", 108, (3, 128 * 128), BF16)
    dexp = at("dexp", 140, (128, INTER), BF16)
    s_scan = at("s_scan", 156, (128, INTER), F32)
    S_in = at("S_in", 0, (128, INTER), F32)
    Sjb = at("Sjb", 108, (128, INTER), F32)
    sq_rms = at("sq_rms", 140, (128, INTER), BF16)
    crel = at("crel", 64, (128, G * 128), BF16)
    scslab = at("scslab", 72, (128, 16 * PH), BF16)
    ygat = at("ygat", 0, (128, KI * T), BF16)
    ynchunk = at("ynchunk", 128, (128, INTER), BF16)
    gchunk = at("gchunk", 144, (128, 64 * 128), BF16)
    sgchunk = at("sgchunk", 160, (128, 64 * 128), BF16)
    xtileM = [at(f"xtileM{i}", 128 + 4 * i, (128, T), F32) for i in range(3)]
    ftileM = [at(f"ftileM{i}", 140 + 4 * i, (128, T), F32) for i in range(2)]

    pc0 = ctx.enter_context(nc.psum_tensor("pc0", [128, 1024], F32))
    pc1 = ctx.enter_context(nc.psum_tensor("pc1", [128, 1024], F32))
    pb4 = ctx.enter_context(nc.psum_tensor("pb4", [128, 512], F32))
    pb5 = ctx.enter_context(nc.psum_tensor("pb5", [128, 512], F32))
    pt0 = ctx.enter_context(nc.psum_tensor("pt0", [128, 1024], BF16))
    pt1 = ctx.enter_context(nc.psum_tensor("pt1", [128, 1024], BF16))
    pcs = [pc0, pc1]

    sWr = [Ctr(ctx.enter_context(nc.semaphore(f"sWr{i}"))) for i in range(2)]
    sLa = [Ctr(ctx.enter_context(nc.semaphore(f"sLa{i}"))) for i in range(3)]
    sLc = [Ctr(ctx.enter_context(nc.semaphore(f"sLc{i}"))) for i in range(3)]
    sLb = [Ctr(ctx.enter_context(nc.semaphore(f"sLb{i}"))) for i in range(3)]
    sSa = [Ctr(ctx.enter_context(nc.semaphore(f"sSa{i}"))) for i in range(3)]
    sSb = [Ctr(ctx.enter_context(nc.semaphore(f"sSb{i}"))) for i in range(3)]
    sB = ctx.enter_context(nc.semaphore("sB"))
    sW = Ctr(ctx.enter_context(nc.semaphore("sW")))
    sWF = Ctr(ctx.enter_context(nc.semaphore("sWF")))
    sM = Ctr(ctx.enter_context(nc.semaphore("sM")))
    sE = Ctr(ctx.enter_context(nc.semaphore("sE")))
    sX = Ctr(ctx.enter_context(nc.semaphore("sX")))
    sSt = Ctr(ctx.enter_context(nc.semaphore("sSt")))
    sV = Ctr(ctx.enter_context(nc.semaphore("sV")))
    sS = Ctr(ctx.enter_context(nc.semaphore("sS")))
    sT = Ctr(ctx.enter_context(nc.semaphore("sT")))
    sG = Ctr(ctx.enter_context(nc.semaphore("sG")))
    sC = Ctr(ctx.enter_context(nc.semaphore("sC")))

    with ctx, nc.Block() as block:

        barK = [0]

        def barrier():
            barK[0] += 1
            tgt = 5 * barK[0]
            for sec in (block.tensor, block.vector, block.scalar,
                        block.gpsimd, block.sync):
                @sec
                def _(e):
                    e.drain()
                    e.nop().then_inc(sB, 1)
                    e.wait_ge(sB, tgt)

        def fence(eng):
            for c in sSa + sSb + [sSt, sX]:
                wat(eng, c)

        @block.gpsimd
        def _(g):
            for dst, src in [(c_idb, id_bf), (c_idf, id_f32), (c_msu, masksu)]:
                inc(g.dma_start(out=dst[:, :], in_=src.ap()), sW, 16)
            inc(g.dma_start(out=c_gatew[:, :], in_=wgate.ap()), sW, 16)
            for dst, src in [(c_dtb, dtbias), (c_normw, normw),
                             (c_gam, gam), (c_bet, bet),
                             (c_nmask, nmask), (c_hmask, hmask)]:
                inc(g.dma_start(out=dst[:, :], in_=src.ap()), sW, 16)
            inc(g.dma_start(out=c_dcolb[:, :], in_=dcol.ap()), sW, 16)
            inc(g.dma_start(out=t128a[:, 0:1], in_=alog.ap()), sW, 16)
            g.memset(c_ones1b[:, :], 1.0)
            g.memset(c_ones1f[:, :], 1.0)
            g.memset(c_onescb[:, :], 1.0)
            g.memset(c_ones3[:, :], 1.0)
            g.memset(onecol[:, :], 1.0)
            g.memset(c_eps[:, :], EPS)
            g.memset(c_inv8k[:, :], 1.0 / INTER)
            inc(g.nop(), sG, 1)

        @block.scalar
        def _(s):
            wat(s, sW)
            wat(s, sG)
            inc(s.activation(t128b[:, 0:1], t128a[:, 0:1], AF.Exp), sS, 1)

        @block.vector
        def _(v):
            wat(v, sS)
            inc(v.tensor_scalar_mul(c_acol[:, :], t128b[:, 0:1], -1.0), sV, 1)
        barrier()

        def ln_stats(src_dram, n_kt, cast):
            q = block.gpsimd if cast else block.sync
            fam = sLa if cast else sLc
            bS, bT, bG = sS.v, sT.v, sG.v
            lv = []

            @q
            def _(g):
                fence(g)
                for kt in range(n_kt):
                    if kt >= 3:
                        wat(g, sG, bG + kt - 2)
                    inc(g.dma_start(
                        out=btile[kt % 3][:, :],
                        in_=src_dram.ap()[kt * 128:(kt + 1) * 128, :]),
                        fam[kt % 3], 16)
                    lv.append(fam[kt % 3].v)

            @block.scalar
            def _(s):
                for kt in range(n_kt):
                    wat(s, fam[kt % 3], lv[kt])
                    if kt >= 1:
                        wat(s, sT, bT + kt)
                    inc(s.activation(sqt[:, :], btile[kt % 3][:, :],
                                     AF.Square), sS, 1)

            @block.tensor
            def _(t):
                for kt in range(n_kt):
                    wat(t, sS, bS + kt + 1)
                    bt = btile[kt % 3]
                    st = (kt == 0)
                    sp = (kt == n_kt - 1)
                    t.matmul(pc0[0:1, 0:512], c_onescb[:, 0:1], bt[:, 0:512],
                             start=st, stop=sp)
                    inc(t.matmul(pc0[0:1, 512:1024], c_onescb[:, 0:1],
                                 bt[:, 512:1024], start=st, stop=sp), sG, 1)
                    t.matmul(pb4[0:1, 0:512], c_onescb[:, 0:1], sqt[:, 0:512],
                             start=st, stop=sp)
                    inc(t.matmul(pb5[0:1, 0:512], c_onescb[:, 0:1],
                                 sqt[:, 512:1024], start=st, stop=sp), sT, 1)

        def ln_finalize(nfeat):
            @block.vector
            def _(v):
                wat(v, sT)
                inc(v.tensor_scalar_mul(r_mu[0:1, 0:512], pc0[0:1, 0:512],
                                        1.0 / nfeat), sV, 1)
                inc(v.tensor_scalar_mul(r_mu[0:1, 512:1024],
                                        pc0[0:1, 512:1024],
                                        1.0 / nfeat), sV, 1)
                inc(v.tensor_scalar_mul(r_rs[0:1, 0:512], pb4[0:1, 0:512],
                                        1.0 / nfeat), sV, 1)
                inc(v.tensor_scalar_mul(r_rs[0:1, 512:1024], pb5[0:1, 0:512],
                                        1.0 / nfeat), sV, 1)
                wat(v, sV)
                inc(v.tensor_mul(l_sb[0:1, 0:1024], r_mu[0:1, :],
                                 r_mu[0:1, :]), sV, 1)
                wat(v, sV)
                inc(v.tensor_sub(r_rs[0:1, :], r_rs[0:1, :],
                                 l_sb[0:1, 0:1024]), sV, 1)

            @block.scalar
            def _(s):
                wat(s, sV)
                inc(s.activation(r_rs[0:1, :], r_rs[0:1, :], AF.Ln, bias=EPS),
                    sS, 1)
                wat(s, sS)
                inc(s.activation(r_rs[0:1, :], r_rs[0:1, :], AF.Exp,
                                 scale=-0.5), sS, 1)

            @block.tensor
            def _(t):
                wat(t, sS)
                mm = None
                for half in range(2):
                    cs = slice(half * 512, (half + 1) * 512)
                    t.matmul(pc0[:, cs], c_ones1f[0:1, :], r_mu[0:1, cs],
                             start=True, stop=True)
                    mm = t.matmul(pc1[:, cs], c_ones1f[0:1, :], r_rs[0:1, cs],
                                  start=True, stop=True)
                inc(mm, sT, 1)

            @block.vector
            def _(v):
                wat(v, sT)
                inc(v.tensor_copy(b_mu[:, :], pc0[:, :]), sV, 1)
                inc(v.tensor_copy(b_rs[:, :], pc1[:, :]), sV, 1)

        def ln_apply(src_dram, dst_reg, n_kt):
            bV = sV.v
            lv = []

            @block.sync
            def _(sy):
                fence(sy)
                for kt in range(n_kt):
                    if kt >= 3:
                        wat(sy, sV, bV + 2 * (kt - 2))
                    inc(sy.dma_start(
                        out=xtile[kt % 3][:, :],
                        in_=src_dram.ap()[kt * 128:(kt + 1) * 128, :]),
                        sLb[kt % 3], 16)
                    lv.append(sLb[kt % 3].v)

            @block.vector
            def _(v):
                for kt in range(n_kt):
                    wat(v, sLb[kt % 3], lv[kt])
                    if kt == 0:
                        wat(v, sV, bV)
                    if kt >= 2:
                        wat(v, sV, bV + 2 * (kt - 1))
                    ft = ftile[kt % 2]
                    inc(v.tensor_sub(ft[:, :], xtile[kt % 3][:, :],
                                     b_mu[:, :]), sV, 1)
                    wat(v, sV)
                    inc(v.tensor_mul(dst_reg[:, kt * T:(kt + 1) * T],
                                     ft[:, :], b_rs[:, :]), sV, 1)

        def mm_stream(w_dram, n_out, n_k, rhs_reg, evac, fc2_interleave=False):
            K = n_k * 128
            nch = K // WCH
            nf = (KL * 128) // WCH if fc2_interleave else 0
            per = nch + nf
            bWF, bE, bM = sWF.v, sE.v, sM.v
            wv = []

            @block.gpsimd
            def _(g):
                cc = 0
                for i in range(n_out):
                    for c in range(per):
                        if cc >= 2:
                            wat(g, sWF, bWF + cc - 1)
                        if c < nch:
                            srcw = w_dram.ap()[i, :, c * WCH:(c + 1) * WCH]
                        else:
                            f = c - nch
                            srcw = wfc2.ap()[i, :, f * WCH:(f + 1) * WCH]
                        inc(g.dma_start(out=wring[cc % 2][:, :], in_=srcw),
                            sWr[cc % 2], 16)
                        wv.append(sWr[cc % 2].v)
                        cc += 1

            bWFt = sWF.v

            @block.tensor
            def _(t):
                cc = 0
                for i in range(n_out):
                    ps = pcs[i % 2]
                    if i >= 2:
                        wat(t, sE, bE + i - 1)
                    first = [True, True]
                    for c in range(per):
                        wat(t, sWr[cc % 2], wv[cc])
                        reg = rhs_reg if c < nch else moereg2
                        kb = c * (WCH // 128) if c < nch \
                            else (c - nch) * (WCH // 128)
                        mm = None
                        for half in range(2):
                            pss = ps[:, half * 512:(half + 1) * 512]
                            for kk in range(WCH // 128):
                                kt = kb + kk
                                last = (c == per - 1 and kk == WCH // 128 - 1)
                                mm = t.matmul(
                                    pss,
                                    wring[cc % 2][:, kk * 128:(kk + 1) * 128],
                                    reg[:, kt * T + half * 512:
                                        kt * T + half * 512 + 512],
                                    start=first[half], stop=last)
                                first[half] = False
                        inc(mm, sWF, 1)
                        cc += 1

            for i in range(n_out):
                evac(i, pcs[i % 2], (sWF, bWFt + per * (i + 1)))

        def emit_ffn(x_src, x_dst, half_factor, final):
            ln_stats(x_src, KD, cast=True)
            ln_finalize(D)
            if _FSTOP <= 1:
                barrier()
                return
            ln_apply(x_src, hreg, KD)
            barrier()
            if _FSTOP <= 2:
                return

            @block.tensor
            def _(t):
                mm = None
                for half in range(2):
                    dst = pb4[0:8, 0:512] if half == 0 else pb5[0:8, 0:512]
                    for kt in range(KD):
                        mm = t.matmul(dst, c_gatew[:, kt * 8:(kt + 1) * 8],
                                      hreg[:, kt * T + half * 512:
                                           kt * T + half * 512 + 512],
                                      start=(kt == 0), stop=(kt == KD - 1))
                inc(mm, sT, 1)

            @block.vector
            def _(v):
                wat(v, sT)
                inc(v.tensor_copy(l_sb[0:8, 0:512], pb4[0:8, 0:512]), sV, 1)
                inc(v.tensor_copy(l_sb[0:8, 512:1024], pb5[0:8, 0:512]), sV, 1)

            @block.tensor
            def _(t):
                wat(t, sV)
                mm = None
                for j in range(8):
                    mm = t.transpose(pc0[:, j * 8:(j + 1) * 8],
                                     l_sb[0:8, j * 128:(j + 1) * 128],
                                     c_idf[0:8, 0:8])
                inc(mm, sT, 1)

            @block.vector
            def _(v):
                wat(v, sT)
                inc(v.tensor_copy(lt_sb[:, :], pc0[:, 0:64]), sV, 1)
                mrot = t128a[:, 0:64]
                inc(v.memset(mrot, -1.0e30), sV, 1)
                for r in range(1, 8):
                    for j in range(8):
                        a = slice(j * 8, j * 8 + 8 - r)
                        bb = slice(j * 8 + r, j * 8 + 8)
                        wat(v, sV)
                        inc(v.tensor_add(t128b[:, 0:8 - r], lt_sb[:, a],
                                         lt_sb[:, bb]), sV, 1)
                        wat(v, sV)
                        inc(v.tensor_max(mrot[:, a], mrot[:, a],
                                         t128b[:, 0:8 - r]), sV, 1)
                for j in range(7):
                    wat(v, sV)
                    inc(v.tensor_reduce(rwcol[:, j:j + 1],
                                        mrot[:, j * 8:(j + 1) * 8], AX.X,
                                        ALU.max), sV, 1)
                wat(v, sV)
                inc(v.tensor_reduce(rwcol[:, 7:8], mrot[:, 56:64], AX.X,
                                    ALU.max), sV, 1)

            @block.tensor
            def _(t):
                wat(t, sV)
                mm = None
                for j in range(8):
                    mm = t.transpose(pc1[0:1, j * 128:(j + 1) * 128],
                                     rwcol[:, j:j + 1], c_idf[:, :])
                inc(mm, sT, 1)

            @block.vector
            def _(v):
                wat(v, sT)
                inc(v.tensor_copy(r_mu[0:1, :], pc1[0:1, :]), sV, 1)

            @block.tensor
            def _(t):
                wat(t, sV)
                mm = None
                for half in range(2):
                    cs = slice(half * 512, (half + 1) * 512)
                    mm = t.matmul(pc0[:, cs], c_ones1f[0:1, :], r_mu[0:1, cs],
                                  start=True, stop=True)
                inc(mm, sT, 1)

            @block.vector
            def _(v):
                wat(v, sT)
                inc(v.tensor_copy(rwb[:, :], pc0[:, :]), sV, 1)
            barrier()
            if _FSTOP <= 3:
                return

            def ev_lat(i, ps, mval):
                @block.vector
                def _(v):
                    wat(v, mval[0], mval[1])
                    inc(v.tensor_copy(latreg[:, i * T:(i + 1) * T], ps[:, :]),
                        sE, 1)
            mm_stream(wfc1, KL, KD, hreg, ev_lat)
            barrier()
            if _FSTOP <= 4:
                return

            bE_moe = sE.v

            def ev_moe(i, ps, mval):
                @block.vector
                def _(v):
                    wat(v, mval[0], mval[1])
                    inc(v.tensor_mul(moereg[:, i * T:(i + 1) * T], ps[:, :],
                                     rwb[:, :]), sE, 1)
                @block.sync
                def _(sy):
                    wat(sy, sE, bE_moe + i + 1)
                    inc(sy.dma_start(out=moe_s.ap()[i * 128:(i + 1) * 128, :],
                                     in_=moereg[:, i * T:(i + 1) * T]), sSt, 16)
            mm_stream(wexp, KL, KL, latreg, ev_moe)
            barrier()
            if _FSTOP <= 5:
                return

            bE_up = sE.v
            bSt_up = sSt.v

            def ev_up(i, ps, mval):
                @block.vector
                def _(v):
                    wat(v, mval[0], mval[1])
                    if i >= 2:
                        wat(v, sE, bE_up + i - 1)
                    inc(v.tensor_scalar_max(etile[i % 2][:, :], ps[:, :], 0.0),
                        sV, 1)
                @block.scalar
                def _(s):
                    wat(s, sV)
                    if i == 16:
                        wat(s, sSt, bSt_up)
                    inc(s.activation(upreg[:, i * T:(i + 1) * T],
                                     etile[i % 2][:, :], AF.Square), sE, 1)
            mm_stream(wup, KI, KD, hreg, ev_up)
            barrier()
            if _FSTOP <= 6:
                return

            @block.sync
            def _(sy):
                fence(sy)
                inc(sy.dma_start(
                    out=moereg2[:, :].rearrange("p (kt t) -> p kt t", t=T),
                    in_=moe_s.ap().rearrange("(kt p) t -> p kt t", p=128)),
                    sX, 16)
                wat(sy, sX)
            barrier()

            bE_dn = sE.v
            dn_store = {}

            def ev_down(i, ps, mval):
                @block.vector
                def _(v):
                    wat(v, mval[0], mval[1])
                    if i >= 2:
                        wat(v, sSa[i % 2], dn_store[i - 2])
                    inc(v.tensor_copy(otileD[i % 2][:, :], ps[:, :]), sE, 1)
                @block.sync
                def _(sy):
                    wat(sy, sE, bE_dn + i + 1)
                    inc(sy.dma_start(out=fo_s.ap()[i * 128:(i + 1) * 128, :],
                                     in_=otileD[i % 2][:, :]), sSa[i % 2], 16)
                    dn_store[i] = sSa[i % 2].v
            mm_stream(wdown, KD, KI, upreg, ev_down, fc2_interleave=True)
            barrier()
            if _FSTOP <= 7:
                return

            ln_stats(fo_s, KD, cast=False)
            ln_finalize(D)
            barrier()

            bV2, bS2 = sV.v, sS.v
            bSbv = [s.v for s in sSb]
            bLcv = [s.v for s in sLc]
            bLbv = [s.v for s in sLb]
            lva = [bLcv[kt % 3] + 16 * (kt // 3 + 1) for kt in range(KD)]
            lvb = [bLbv[kt % 3] + 16 * (kt // 3 + 1) for kt in range(KD)]

            sttv = []
            @block.vector
            def _(v):
                for kt in range(KD):
                    wat(v, sLc[kt % 3], lva[kt])
                    wat(v, sLb[kt % 3], lvb[kt])
                    if kt == 0:
                        wat(v, sV, bV2)
                    if kt >= 2:
                        wat(v, sS, bS2 + kt - 1)
                        wat(v, sV, sttv[kt - 2])
                    ft = ftile[kt % 2]
                    inc(v.tensor_sub(ft[:, :], btile[kt % 3][:, :],
                                     b_mu[:, :]), sV, 1)
                    wat(v, sV)
                    inc(v.tensor_mul(ft[:, :], ft[:, :], b_rs[:, :]), sV, 1)
                    if kt >= 1:
                        wat(v, sS, bS2 + kt)
                        inc(v.scalar_tensor_tensor(
                            xtile[(kt - 1) % 3][:, :], ftile[(kt - 1) % 2][:, :],
                            half_factor, xtile[(kt - 1) % 3][:, :],
                            ALU.mult, ALU.add), sV, 1)
                        sttv.append(sV.v)
                wat(v, sS, bS2 + KD)
                inc(v.scalar_tensor_tensor(
                    xtile[(KD - 1) % 3][:, :], ftile[(KD - 1) % 2][:, :],
                    half_factor, xtile[(KD - 1) % 3][:, :],
                    ALU.mult, ALU.add), sV, 1)
                sttv.append(sV.v)

            @block.scalar
            def _(s):
                for kt in range(KD):
                    nstt = max(0, kt - 1)
                    wat(s, sV, bV2 + 2 * (kt + 1) + nstt)
                    inc(s.activation(ftile[kt % 2][:, :], ftile[kt % 2][:, :],
                                     AF.Identity, bias=c_bet[:, kt:kt + 1],
                                     scale=c_gam[:, kt:kt + 1]), sS, 1)

            @block.sync
            def _(sy):
                fence(sy)
                for kt in range(KD):
                    if kt >= 3:
                        wat(sy, sSb[kt % 3],
                            bSbv[kt % 3] + 16 * ((kt - 3) // 3 + 1))
                    inc(sy.dma_start(out=btile[kt % 3][:, :],
                                     in_=fo_s.ap()[kt * 128:(kt + 1) * 128, :]),
                        sLc[kt % 3], 16)
                    inc(sy.dma_start(
                        out=xtile[kt % 3][:, :],
                        in_=x_src.ap()[kt * 128:(kt + 1) * 128, :]),
                        sLb[kt % 3], 16)
                    if kt >= 1:
                        wat(sy, sV, sttv[kt - 1])
                        inc(sy.dma_start(
                            out=x_dst.ap()[(kt - 1) * 128:kt * 128, :],
                            in_=xtile[(kt - 1) % 3][:, :]), sSb[(kt - 1) % 3],
                            16)
                wat(sy, sV, sttv[KD - 1])
                inc(sy.dma_start(
                    out=x_dst.ap()[(KD - 1) * 128:KD * 128, :],
                    in_=xtile[(KD - 1) % 3][:, :]), sSb[(KD - 1) % 3], 16)
            if final:
                @block.sync
                def _(sy):
                    fence(sy)
            barrier()

        def ssd_chunk(c):
            bX, bV, bS, bT = sX.v, sV.v, sS.v, sT.v

            @block.sync
            def _(sy):
                fence(sy)
                inc(sy.dma_start(
                    out=hbc[:, :].rearrange("p (j l) -> p j l", l=128),
                    in_=conv_s.ap().rearrange("(j p) t -> p j t", p=128)
                    [:, :, c * 128:(c + 1) * 128]), sX, 16)
                inc(sy.dma_start(
                    out=acstack[0:3, :],
                    in_=dataclasses.replace(
                        acflat_s.ap(), offset=c * 128 * 128,
                        ap=[[NCH * 128 * 128, 3], [1, 128 * 128]])), sX, 16)

            @block.vector
            def _(v):
                wat(v, sX, bX + 32)
                if c == 0:
                    inc(v.tensor_scalar_mul(col_a[:, :], onecol[:, :], 0.0),
                        sV, 1)
                else:
                    inc(v.tensor_copy(col_a[:, :],
                                      acum[:, c * 128 - 1:c * 128]), sV, 1)
                wat(v, sV)
                inc(v.tensor_single_scalar(t128a[:, :],
                                           acum[:, c * 128:(c + 1) * 128],
                                           col_a[:, 0:1], ALU.subtract), sV, 1)
                wat(v, sV)
                inc(v.tensor_copy(col_b[:, :], t128a[:, 127:128]), sV, 1)
                wat(v, sV)
                inc(v.tensor_single_scalar(t128b[:, :], t128a[:, :],
                                           col_b[:, 0:1], ALU.subtract), sV, 1)

            @block.scalar
            def _(s):
                wat(s, sV)
                inc(s.activation(eacm[:, :], t128a[:, :], AF.Exp), sS, 1)
                inc(s.activation(t128c[:, :], t128b[:, :], AF.Exp, scale=-1.0),
                    sS, 1)
                inc(s.activation(col_c[:, :], col_b[:, :], AF.Exp), sS, 1)

            @block.vector
            def _(v):
                wat(v, sS)
                inc(v.tensor_mul(t128b[:, :], t128c[:, :],
                                 dtp[:, c * 128:(c + 1) * 128]), sV, 1)

            @block.tensor
            def _(t):
                wat(t, sV)
                t.transpose(pc0[:, 0:128], t128b[:, :], c_idf[:, :])
                inc(t.transpose(pc1[0:1, 0:128], col_c[:, :], c_idf[:, :]),
                    sT, 1)

            @block.vector
            def _(v):
                wat(v, sT)
                inc(v.tensor_copy(ddtm[:, :], pc0[:, 0:128]), sV, 1)
                inc(v.tensor_copy(rowbuf[0:1, :], pc1[0:1, 0:128]), sV, 1)

            @block.tensor
            def _(t):
                wat(t, sV)
                inc(t.matmul(pc0[:, 0:128], c_ones1f[0:1, :], rowbuf[0:1, :],
                             start=True, stop=True), sT, 1)

            @block.vector
            def _(v):
                wat(v, sT)
                inc(v.tensor_copy(dchbc[:, :], pc0[:, 0:128]), sV, 1)

            bT2, bV2 = sT.v, sV.v
            @block.tensor
            def _(t):
                wat(t, sV)
                for j in range(72):
                    if j >= 2:
                        wat(t, sV, bV2 + j - 1)
                    pt = pt0 if j % 2 == 0 else pt1
                    srcj = hbc[:, j * 128:(j + 1) * 128]
                    inc(t.transpose(pt[:, 0:128], srcj, c_idb[:, :]), sT, 1)

            @block.vector
            def _(v):
                for j in range(72):
                    wat(v, sT, bT2 + j + 1)
                    pt = pt0 if j % 2 == 0 else pt1
                    if j < 64:
                        dst = x_tm[:, j * 128:(j + 1) * 128]
                    else:
                        dst = b_tm[:, (j - 64) * 128:(j - 63) * 128]
                    inc(v.tensor_copy(dst, pt[:, 0:128]), sV, 1)

            bT3, bS3 = sT.v, sS.v
            @block.tensor
            def _(t):
                wat(t, sV)
                for g in range(G):
                    if g >= 2:
                        wat(t, sS, bS3 + g - 1)
                    inc(t.matmul(pb4[:, 0:128] if g % 2 == 0 else pb5[:, 0:128],
                                 hbc[:, (64 + g) * 128:(64 + g + 1) * 128],
                                 hbc[:, (72 + g) * 128:(72 + g + 1) * 128],
                                 start=True, stop=True), sT, 1)

            @block.scalar
            def _(s):
                for g in range(G):
                    wat(s, sT, bT3 + g + 1)
                    src = pb4 if g % 2 == 0 else pb5
                    inc(s.copy(gtm[:, g * 128:(g + 1) * 128], src[:, 0:128]),
                        sS, 1)

            for g in range(G):
                h0 = 16 * g
                bT4, bS4, bV4 = sT.v, sS.v, sV.v

                @block.tensor
                def _(t):
                    wat(t, sV)
                    wat(t, sS)
                    for sub in range(2):
                        hh = h0 + 8 * sub
                        ps = pc0 if sub == 0 else pc1
                        mm = None
                        for half in range(2):
                            pss = ps[:, half * 512:(half + 1) * 512]
                            hsl = slice(hh + 4 * half, hh + 4 * half + 4)
                            first = True
                            for sp, nacv in enumerate([nac_hi, nac_lo,
                                                       nac_lo2]):
                                t.matmul(pss,
                                         nacv[:, c * 128:(c + 1) * 128],
                                         rep0(c_idb[:, hsl], 128),
                                         start=first, stop=False)
                                first = False
                            t.matmul(pss, c_ones3[0:3, :],
                                     acstack[0:3, (hh + 4 * half) * 128:
                                             (hh + 4 * half + 4) * 128],
                                     start=False, stop=False)
                            mm = t.matmul(pss, c_idb[:, :],
                                          mid0(c_msu[:, :], 4),
                                          start=False, stop=True)
                        inc(mm, sT, 1)

                @block.scalar
                def _(s):
                    for sub in range(2):
                        wat(s, sT, bT4 + sub + 1)
                        ps = pc0 if sub == 0 else pc1
                        inc(s.activation(expd[:, sub * 1024:(sub + 1) * 1024],
                                         ps[:, :], AF.Exp, scale=-1.0), sS, 1)

                @block.vector
                def _(v):
                    wat(v, sS, bS4 + 2)
                    for j in range(16):
                        h = h0 + j
                        inc(v.scalar_tensor_tensor(
                            mtsb[:, j * 128:(j + 1) * 128],
                            expd[:, j * 128:(j + 1) * 128],
                            dtT[:, c * 128 + h:c * 128 + h + 1],
                            gtm[:, g * 128:(g + 1) * 128],
                            ALU.mult, ALU.mult), sV, 1)
                        inc(v.tensor_single_scalar(
                            xdd[:, j * PH:(j + 1) * PH],
                            x_tm[:, h * PH:(h + 1) * PH],
                            ddtm[:, h:h + 1], ALU.mult), sV, 1)
                    inc(v.tensor_copy(sbfb[:, :],
                                      s_scan[:, h0 * PH:(h0 + 16) * PH]),
                        sV, 1)
                    inc(v.tensor_mul(xdtile[:, :],
                                     x_tm[:, h0 * PH:(h0 + 16) * PH],
                                     dexp[:, h0 * PH:(h0 + 16) * PH]), sV, 1)

                bT5 = sT.v
                @block.tensor
                def _(t):
                    wat(t, sV)
                    wat(t, sS)
                    mm = None
                    for j in range(16):
                        mm = t.matmul(pc0[:, j * PH:(j + 1) * PH],
                                      hbc[:, (72 + g) * 128:(72 + g + 1) * 128],
                                      sbfb[:, j * PH:(j + 1) * PH],
                                      start=True, stop=True)
                    inc(mm, sT, 1)
                    for j in range(16):
                        t.matmul(pc1[:, j * PH:(j + 1) * PH],
                                 mtsb[:, j * 128:(j + 1) * 128],
                                 x_tm[:, (h0 + j) * PH:(h0 + j + 1) * PH],
                                 start=True, stop=False)
                        mm = t.matmul(pc1[:, j * PH:(j + 1) * PH],
                                      c_idb[:, :],
                                      xdtile[:, j * PH:(j + 1) * PH],
                                      start=False, stop=True)
                    inc(mm, sT, 1)
                    for half in range(2):
                        hsl = slice(h0 + 8 * half, h0 + 8 * half + 8)
                        dst = pb4 if half == 0 else pb5
                        mm = t.matmul(dst[:, 0:512],
                                      eacm[:, :], rep0(c_idb[:, hsl], PH),
                                      start=True, stop=True)
                    inc(mm, sT, 1)
                    wat(t, sS, sS.v + 2)
                    for j in range(16):
                        dst = pb4 if j < 8 else pb5
                        mm = t.matmul(dst[:, (j % 8) * PH:(j % 8 + 1) * PH],
                                      b_tm[:, g * 128:(g + 1) * 128],
                                      xdd[:, j * PH:(j + 1) * PH],
                                      start=True, stop=True)
                    inc(mm, sT, 1)

                @block.scalar
                def _(s):
                    wat(s, sT, bT5 + 3)
                    inc(s.copy(easb[:, 0:512], pb4[:, 0:512]), sS, 1)
                    inc(s.copy(easb[:, 512:1024], pb5[:, 0:512]), sS, 1)

                @block.vector
                def _(v):
                    wat(v, sS)
                    inc(v.tensor_mul(ssd_t1[:, :], pc0[:, :], easb[:, :]),
                        sV, 1)
                    wat(v, sV)
                    inc(v.tensor_add(y_sb[:, h0 * PH:(h0 + 16) * PH],
                                     pc1[:, :], ssd_t1[:, :]), sV, 1)
                    wat(v, sT)
                    for j in range(16):
                        h = h0 + j
                        src = pb4 if j < 8 else pb5
                        inc(v.scalar_tensor_tensor(
                            s_scan[:, h * PH:(h + 1) * PH],
                            s_scan[:, h * PH:(h + 1) * PH],
                            dchbc[:, h:h + 1],
                            src[:, (j % 8) * PH:(j % 8 + 1) * PH],
                            ALU.mult, ALU.add), sV, 1)

            @block.sync
            def _(sy):
                wat(sy, sV)
                inc(sy.dma_start(out=yraw_s.ap()[c * 128:(c + 1) * 128, :],
                                 in_=y_sb[:, :]), sSt, 16)
            @block.sync
            def _(sy):
                wat(sy, sSt)

        def s_ffn1():
            emit_ffn(xT, x1_s, 0.5, False)


        def s_mln():
            ln_stats(x1_s, KD, cast=True)
            ln_finalize(D)
            ln_apply(x1_s, hreg, KD)
            barrier()


        def s_inproj():
            def logical_mt(j):
                if j == 0:
                    return 144
                if j <= MC:
                    return 64 + (j - 1)
                return j - 81

            bE_ip = sE.v
            st_rec = {}

            def ev_inp(i, ps, mval):
                mt = logical_mt(i)
                if mt == 144:
                    @block.vector
                    def _(v):
                        wat(v, mval[0], mval[1])
                        inc(v.tensor_copy(dtraw[:, :], ps[:, :]), sE, 1)
                else:
                    @block.vector
                    def _(v):
                        wat(v, mval[0], mval[1])
                        if i - 2 in st_rec:
                            wat(v, sSa[i % 2], st_rec[i - 2])
                        if 64 <= mt < 144:
                            inc(v.tensor_copy(otileI[i % 2][:, :], ps[:, :]), sV, 1)
                            wat(v, sV)
                            jj = mt - 64
                            inc(v.tensor_copy(tails[:, jj * 3:(jj + 1) * 3],
                                              otileI[i % 2][:, 1021:1024]), sE, 1)
                        else:
                            inc(v.tensor_copy(otileI[i % 2][:, :], ps[:, :]), sE, 1)
                    @block.sync
                    def _(sy):
                        wat(sy, sE, bE_ip + i + 1)
                        inc(sy.dma_start(
                            out=proj_s.ap()[mt * 128:(mt + 1) * 128, :],
                            in_=otileI[i % 2][:, :]), sSa[i % 2], 16)
                        st_rec[i] = sSa[i % 2].v
            mm_stream(winp, MP, KD, hreg, ev_inp)
            barrier()


        def s_ag1():
            @block.sync
            def _(sy):
                inc(sy.dma_start(out=ag1i.ap(), in_=tails[:, :]), sSt, 16)
                wat(sy, sSt)

            @block.gpsimd
            def _(g):
                wat(g, sSt)
                inc(g.collective_compute(
                    "AllGather", ALU.bypass,
                    replica_groups=[[0, 1, 2, 3], [4, 5, 6, 7]],
                    ins=[ag1i.ap().opt()], outs=[ag1o.ap().opt()]), sC, 1)
                wat(g, sC)
                inc(g.nop(), sG, 1)
            barrier()

            bX_t, bV_t = sX.v, sV.v

            @block.sync
            def _(sy):
                for j in range(4):
                    if j >= 1:
                        wat(sy, sV, bV_t + j + 1)
                    inc(sy.dma_start(out=tails[:, :],
                                     in_=ag1o.ap()[j * 128:(j + 1) * 128, :]),
                        sX, 16)

            @block.vector
            def _(v):
                inc(v.memset(tailp[:, :], 0.0), sV, 1)
                for j in range(4):
                    wat(v, sX, bX_t + 16 * (j + 1))
                    wat(v, sV)
                    inc(v.scalar_tensor_tensor(tailp[:, :], tails[:, :],
                                               c_nmask[:, j:j + 1], tailp[:, :],
                                               ALU.mult, ALU.add), sV, 1)
            barrier()


        def s_m2a():
            @block.vector
            def _(v):
                inc(v.tensor_single_scalar(zbuf[:, :], dtraw[:, :], c_dtb[:, 0:1],
                                           ALU.add), sV, 1)

            @block.scalar
            def _(s):
                wat(s, sV)
                inc(s.activation(adtb[:, :], zbuf[:, :], AF.Abs), sS, 1)
                wat(s, sS)
                inc(s.activation(adtb[:, :], adtb[:, :], AF.Exp, scale=-1.0), sS, 1)
                wat(s, sS)
                inc(s.activation(adtb[:, :], adtb[:, :], AF.Ln, bias=1.0), sS, 1)

            @block.vector
            def _(v):
                wat(v, sS)
                inc(v.tensor_scalar_max(dtp[:, :], zbuf[:, :], 0.0), sV, 1)
                wat(v, sV)
                inc(v.tensor_add(dtp[:, :], dtp[:, :], adtb[:, :]), sV, 1)
                wat(v, sV)
                inc(v.tensor_single_scalar(adtb[:, :], dtp[:, :], c_acol[:, 0:1],
                                           ALU.mult), sV, 1)
                wat(v, sV)
                inc(v.tensor_tensor_scan(acum[:, :], adtb[:, :], adtb[:, :], 0.0,
                                         ALU.add, ALU.bypass), sV, 1)
                wat(v, sV)
                inc(v.tensor_copy(nac_hi[:, :], acum[:, :]), sV, 1)
                wat(v, sV)
                inc(v.tensor_sub(zbuf[:, :], acum[:, :], nac_hi[:, :]), sV, 1)
                wat(v, sV)
                inc(v.tensor_copy(nac_lo[:, :], zbuf[:, :]), sV, 1)
                wat(v, sV)
                inc(v.tensor_sub(zbuf[:, :], zbuf[:, :], nac_lo[:, :]), sV, 1)
                wat(v, sV)
                inc(v.tensor_copy(nac_lo2[:, :], zbuf[:, :]), sV, 1)

            for sp in range(3):
                src = [nac_hi, nac_lo, nac_lo2][sp]
                bSt_sp = sSt.v

                @block.vector
                def _(v, src=src):
                    wat(v, sV)
                    wat(v, sSt, sSt.v)
                    inc(v.tensor_scalar_mul(tmpb16[:, :], src[:, :], -1.0), sV, 1)

                @block.sync
                def _(sy, sp=sp):
                    wat(sy, sV)
                    for cch in range(NCH):
                        inc(sy.dma_start(
                            out=acflat_s.ap()[sp * NCH + cch].rearrange(
                                "(h l) -> h l", l=128),
                            in_=tmpb16[:, cch * 128:(cch + 1) * 128]), sSt, 16)

            bT_d, bV_d = sT.v, sV.v

            @block.tensor
            def _(t):
                wat(t, sV)
                for cch in range(NCH):
                    if cch >= 2:
                        wat(t, sV, bV_d + cch - 1)
                    ps = pc0 if cch % 2 == 0 else pc1
                    inc(t.transpose(ps[:, 0:128],
                                    dtp[:, cch * 128:(cch + 1) * 128],
                                    c_idf[:, :]), sT, 1)

            @block.vector
            def _(v):
                for cch in range(NCH):
                    wat(v, sT, bT_d + cch + 1)
                    ps = pc0 if cch % 2 == 0 else pc1
                    inc(v.tensor_copy(dtT[:, cch * 128:(cch + 1) * 128],
                                      ps[:, 0:128]), sV, 1)
            barrier()


        def s_dexp():
            bS_pre = sS.v

            @block.tensor
            def _(t):
                bT_e = sT.v
                for half in range(INTER // 512):
                    h0 = half * 8
                    if half >= 1:
                        wat(t, sS, bS_pre + half)
                    inc(t.matmul(pb4[0:1, 0:512], c_dcolb[:, 0:1],
                                 rep0(c_idb[:, h0:h0 + 8], PH),
                                 start=True, stop=True), sT, 1)

            bT_e = sT.v - INTER // 512
            @block.scalar
            def _(s):
                wat(s, sSt)
                for half in range(INTER // 512):
                    wat(s, sT, bT_e + half + 1)
                    inc(s.copy(derow[0:1, half * 512:(half + 1) * 512],
                               pb4[0:1, 0:512]), sS, 1)

            bS_e = sS.v - INTER // 512
            bT_e2 = sT.v
            @block.tensor
            def _(t):
                bV_e = sV.v
                for half in range(INTER // 512):
                    wat(t, sS, bS_e + half + 1)
                    if half >= 2:
                        wat(t, sV, bV_e + half - 1)
                    dst = pc0 if half % 2 == 0 else pc1
                    inc(t.matmul(dst[:, 0:512], c_ones1b[0:1, :],
                                 derow[0:1, half * 512:(half + 1) * 512],
                                 start=True, stop=True), sT, 1)

            @block.vector
            def _(v):
                for half in range(INTER // 512):
                    wat(v, sT, bT_e2 + half + 1)
                    srcp = pc0 if half % 2 == 0 else pc1
                    inc(v.tensor_copy(dexp[:, half * 512:(half + 1) * 512],
                                      srcp[:, 0:512]), sV, 1)
            barrier()


        def s_conv():
            @block.sync
            def _(sy):
                inc(sy.dma_start(out=c_convw[:, :], in_=convw.ap()), sX, 16)
                inc(sy.dma_start(out=c_convb[:, :], in_=convb.ap()), sX, 16)
                wat(sy, sX)
            barrier()
            bV_c, bS_c = sV.v, sS.v
            cv_load = []
            bSav = [s.v for s in sSa]

            @block.sync
            def _(sy):
                fence(sy)
                for i in range(MC):
                    if i >= 2:
                        wat(sy, sV, bV_c + 5 * (i - 1))
                    inc(sy.dma_start(
                        out=conv_in[i % 2][:, 3:1027],
                        in_=proj_s.ap()[(64 + i) * 128:(64 + i + 1) * 128, :]),
                        sLc[i % 2], 16)
                    cv_load.append(sLc[i % 2].v)
                    if i >= 1:
                        wat(sy, sS, bS_c + i)
                        inc(sy.dma_start(
                            out=conv_s.ap()[(i - 1) * 128:i * 128, :],
                            in_=otileC[(i - 1) % 2][:, :]), sSa[(i - 1) % 2],
                            16)
                wat(sy, sS, bS_c + MC)
                inc(sy.dma_start(
                    out=conv_s.ap()[(MC - 1) * 128:MC * 128, :],
                    in_=otileC[(MC - 1) % 2][:, :]), sSa[(MC - 1) % 2], 16)

            @block.vector
            def _(v):
                for i in range(MC):
                    wat(v, sLc[i % 2], cv_load[i])
                    if i >= 2:
                        wat(v, sS, bS_c + i - 1)
                    ci = conv_in[i % 2]
                    ca = conv_ac[i % 2]
                    inc(v.tensor_copy(ci[:, 0:3], tailp[:, i * 3:(i + 1) * 3]),
                        sV, 1)
                    wat(v, sV)
                    inc(v.tensor_single_scalar(ca[:, :], ci[:, 0:1024],
                                               c_convw[:, i * 4:i * 4 + 1],
                                               ALU.mult), sV, 1)
                    for k in range(1, 4):
                        wat(v, sV)
                        inc(v.scalar_tensor_tensor(
                            ca[:, :], ci[:, k:k + 1024],
                            c_convw[:, i * 4 + k:i * 4 + k + 1], ca[:, :],
                            ALU.mult, ALU.add), sV, 1)

            @block.scalar
            def _(s):
                for i in range(MC):
                    wat(s, sV, bV_c + 5 * (i + 1))
                    if i >= 2:
                        wat(s, sSa[i % 2], bSav[i % 2] + 16 * ((i - 2) // 2 + 1))
                    inc(s.activation(otileC[i % 2][:, :], conv_ac[i % 2][:, :],
                                     AF.Silu, bias=c_convb[:, i:i + 1]), sS, 1)
            barrier()


        def s_ssd():
            @block.vector
            def _(v):
                inc(v.memset(s_scan[:, :], 0.0), sV, 1)
            barrier()
            for c in range(NCH):
                ssd_chunk(c)
                barrier()

        def s_m4():
            @block.scalar
            def _(s):
                inc(s.activation(col_a[:, :], acum[:, T - 1:T], AF.Exp), sS, 1)

            @block.vector
            def _(v):
                inc(v.memset(t128c[:, 0:63], 0.0), sV, 1)

            @block.sync
            def _(sy):
                wat(sy, sS)
                wat(sy, sV)
                inc(sy.dma_start(out=ag2i.ap()[:, 0:INTER], in_=s_scan[:, :]),
                    sSt, 16)
                inc(sy.dma_start(out=ag2i.ap()[:, INTER:INTER + 1],
                                 in_=col_a[:, :],
                                 allow_slow_non_contiguous=True), sSt, 16)
                inc(sy.dma_start(out=ag2i.ap()[:, INTER + 1:INTER + 64],
                                 in_=t128c[:, 0:63]), sSt, 16)
                wat(sy, sSt)

            @block.gpsimd
            def _(g):
                wat(g, sSt)
                inc(g.collective_compute(
                    "AllGather", ALU.bypass,
                    replica_groups=[[0, 1, 2, 3], [4, 5, 6, 7]],
                    ins=[ag2i.ap().opt()], outs=[ag2o.ap().opt()]), sC, 1)
                wat(g, sC)
                inc(g.nop(), sG, 1)
            barrier()

            bX_s = sX.v

            @block.sync
            def _(sy):
                for j in range(4):
                    inc(sy.dma_start(out=dtotc[j][:, :],
                                     in_=ag2o.ap()[j * 128:(j + 1) * 128,
                                                   INTER:INTER + 1],
                                     allow_slow_non_contiguous=True), sX, 16)

            @block.vector
            def _(v):
                wat(v, sX, bX_s + 64)
                for j in range(1, 4):
                    wat(v, sV)
                    inc(v.tensor_scalar_add(col_a[:, :], dtotc[j][:, :], -1.0),
                        sV, 1)
                    wat(v, sV)
                    inc(v.scalar_tensor_tensor(dtotc[j][:, :], col_a[:, :],
                                               c_hmask[:, j:j + 1], onecol[:, :],
                                               ALU.mult, ALU.add), sV, 1)
                wat(v, sV)
                inc(v.tensor_copy(ucol[3][:, :], c_hmask[:, 3:4]), sV, 1)
                inc(v.tensor_single_scalar(ucol[2][:, :], dtotc[3][:, :],
                                           c_hmask[:, 2:3], ALU.mult), sV, 1)
                wat(v, sV)
                inc(v.tensor_mul(col_b[:, :], dtotc[3][:, :], dtotc[2][:, :]),
                    sV, 1)
                wat(v, sV)
                inc(v.tensor_single_scalar(ucol[1][:, :], col_b[:, :],
                                           c_hmask[:, 1:2], ALU.mult), sV, 1)
                wat(v, sV)
                inc(v.tensor_mul(col_b[:, :], col_b[:, :], dtotc[1][:, :]), sV, 1)
                wat(v, sV)
                inc(v.tensor_single_scalar(ucol[0][:, :], col_b[:, :],
                                           c_hmask[:, 0:1], ALU.mult), sV, 1)
                inc(v.memset(S_in[:, :], 0.0), sV, 1)
            barrier()

            for j in range(4):
                bT_u, bV_u, bX_u = sT.v, sV.v, sX.v

                @block.sync
                def _(sy, j=j):
                    inc(sy.dma_start(out=Sjb[:, :],
                                     in_=ag2o.ap()[j * 128:(j + 1) * 128, 0:INTER]),
                        sX, 16)

                @block.tensor
                def _(t, j=j):
                    inc(t.transpose(pc0[0:1, 0:128], ucol[j][:, :],
                                    c_idf[:, :]), sT, 1)

                @block.vector
                def _(v, j=j):
                    wat(v, sT, bT_u + 1)
                    inc(v.tensor_copy(rowbuf[0:1, :], pc0[0:1, 0:128]), sV, 1)

                @block.tensor
                def _(t, j=j):
                    wat(t, sV, bV_u + 1)
                    inc(t.matmul(pc0[:, 0:128], c_ones1f[0:1, :],
                                 rowbuf[0:1, :], start=True, stop=True),
                        sT, 1)

                @block.vector
                def _(v, j=j):
                    wat(v, sT, bT_u + 2)
                    wat(v, sV, bV_u + 1)
                    inc(v.tensor_copy(pdbc[:, :], pc0[:, 0:128]), sV, 1)
                    wat(v, sV)
                    wat(v, sX, bX_u + 16)
                    for h in range(128):
                        inc(v.scalar_tensor_tensor(
                            S_in[:, h * PH:(h + 1) * PH],
                            Sjb[:, h * PH:(h + 1) * PH],
                            pdbc[:, h:h + 1],
                            S_in[:, h * PH:(h + 1) * PH],
                            ALU.mult, ALU.add), sV, 1)
                barrier()


        def s_m5():
            for c in range(NCH):
                bX_m, bV_m, bS_m, bT_m = sX.v, sV.v, sS.v, sT.v

                @block.sync
                def _(sy, c=c):
                    inc(sy.dma_start(out=y_sb[:, :],
                                     in_=yraw_s.ap()[c * 128:(c + 1) * 128, :]),
                        sX, 16)
                    inc(sy.dma_start(
                        out=crel[:, :].rearrange("p (j l) -> p j l", l=128),
                        in_=conv_s.ap().rearrange("(j p) t -> p j t", p=128)
                        [:, 72:80, c * 128:(c + 1) * 128]), sX, 16)

                @block.vector
                def _(v, c=c):
                    if c == 0:
                        inc(v.tensor_scalar_mul(col_a[:, :], onecol[:, :], 0.0),
                            sV, 1)
                    else:
                        inc(v.tensor_copy(col_a[:, :],
                                          acum[:, c * 128 - 1:c * 128]), sV, 1)
                    wat(v, sV)
                    inc(v.tensor_single_scalar(t128a[:, :],
                                               acum[:, c * 128:(c + 1) * 128],
                                               col_a[:, 0:1], ALU.subtract), sV, 1)

                @block.scalar
                def _(s, c=c):
                    wat(s, sV)
                    inc(s.activation(eacm[:, :], t128a[:, :], AF.Exp), sS, 1)
                    if c == 0:
                        inc(s.copy(col_b[:, :], onecol[:, :]), sS, 1)
                    else:
                        inc(s.activation(col_b[:, :], acum[:, c * 128 - 1:c * 128],
                                         AF.Exp), sS, 1)

                @block.tensor
                def _(t, c=c):
                    wat(t, sS)
                    inc(t.transpose(pc0[0:1, 0:128], col_b[:, :], c_idf[:, :]),
                        sT, 1)

                @block.vector
                def _(v, c=c):
                    wat(v, sT)
                    inc(v.tensor_copy(rowbuf[0:1, :], pc0[0:1, 0:128]), sV, 1)

                @block.tensor
                def _(t, c=c):
                    wat(t, sV)
                    inc(t.matmul(pc0[:, 0:128], c_ones1f[0:1, :],
                                 rowbuf[0:1, :], start=True, stop=True),
                        sT, 1)

                @block.vector
                def _(v, c=c):
                    wat(v, sT)
                    inc(v.tensor_copy(pdbc[:, :], pc0[:, 0:128]), sV, 1)

                for g in range(G):
                    h0 = 16 * g
                    bV_g, bT_g, bS_g = sV.v, sT.v, sS.v

                    @block.vector
                    def _(v, g=g, c=c):
                        wat(v, sX, bX_m + 32)
                        wat(v, sV, bV_g)
                        wat(v, sT, bT_g)
                        for j in range(16):
                            h = h0 + j
                            inc(v.tensor_single_scalar(
                                scslab[:, j * PH:(j + 1) * PH],
                                S_in[:, h * PH:(h + 1) * PH],
                                pdbc[:, h:h + 1], ALU.mult), sV, 1)

                    @block.tensor
                    def _(t, g=g, c=c):
                        wat(t, sV)
                        mmc = None
                        for j in range(16):
                            mmc = t.matmul(pc0[:, j * PH:(j + 1) * PH],
                                           crel[:, g * 128:(g + 1) * 128],
                                           scslab[:, j * PH:(j + 1) * PH],
                                           start=True, stop=True)
                        mm = None
                        for half in range(2):
                            hsl = slice(h0 + 8 * half, h0 + 8 * half + 8)
                            dst = pb4 if half == 0 else pb5
                            mm = t.matmul(dst[:, 0:512],
                                          eacm[:, :], rep0(c_idb[:, hsl], PH),
                                          start=True, stop=True)
                        inc(mm, sT, 1)

                    @block.scalar
                    def _(s, g=g):
                        wat(s, sT)
                        inc(s.copy(easb[:, 0:512], pb4[:, 0:512]), sS, 1)
                        inc(s.copy(easb[:, 512:1024], pb5[:, 0:512]), sS, 1)

                    @block.vector
                    def _(v, g=g):
                        wat(v, sS)
                        inc(v.tensor_mul(ssd_t1[:, :], pc0[:, :], easb[:, :]),
                            sV, 1)
                        wat(v, sV)
                        inc(v.tensor_add(y_sb[:, h0 * PH:(h0 + 16) * PH],
                                         y_sb[:, h0 * PH:(h0 + 16) * PH],
                                         ssd_t1[:, :]), sV, 1)

                @block.scalar
                def _(s, c=c):
                    wat(s, sV)
                    inc(s.activation(sq_rms[:, :], y_sb[:, :], AF.Square), sS, 1)

                @block.vector
                def _(v, c=c):
                    wat(v, sS)
                    inc(v.tensor_reduce(mscol[:, 0:1], sq_rms[:, :], AX.X,
                                        ALU.add), sV, 1)

                @block.scalar
                def _(s, c=c):
                    wat(s, sV)
                    inc(s.activation(mscol[:, 0:1], mscol[:, 0:1], AF.Ln,
                                     bias=EPS, scale=1.0 / INTER), sS, 1)
                    wat(s, sS)
                    inc(s.activation(mscol[:, 0:1], mscol[:, 0:1], AF.Exp,
                                     scale=-0.5), sS, 1)

                @block.vector
                def _(v, c=c):
                    wat(v, sS)
                    inc(v.tensor_single_scalar(y_sb[:, :], y_sb[:, :],
                                               mscol[:, 0:1], ALU.mult), sV, 1)

                @block.sync
                def _(sy, c=c):
                    wat(sy, sV)
                    inc(sy.dma_start(out=ynorm_s.ap()[c * 128:(c + 1) * 128, :],
                                     in_=y_sb[:, :]), sSt, 16)
                    wat(sy, sSt)
                barrier()


        def s_m6():
            for c in range(NCH):
                bX_g, bS_g, bT_g, bV_g = sX.v, sS.v, sT.v, sV.v

                @block.sync
                def _(sy, c=c):
                    inc(sy.dma_start(out=ynchunk[:, :],
                                     in_=ynorm_s.ap()[c * 128:(c + 1) * 128, :]),
                        sX, 16)
                    inc(sy.dma_start(
                        out=gchunk[:, :].rearrange("p (j l) -> p j l", l=128),
                        in_=proj_s.ap().rearrange("(j p) t -> p j t", p=128)
                        [:, 0:64, c * 128:(c + 1) * 128]), sX, 16)

                @block.scalar
                def _(s, c=c):
                    wat(s, sX, bX_g + 32)
                    inc(s.activation(sgchunk[:, 0:4096], gchunk[:, 0:4096],
                                     AF.Silu), sS, 1)
                    inc(s.activation(sgchunk[:, 4096:8192], gchunk[:, 4096:8192],
                                     AF.Silu), sS, 1)

                @block.tensor
                def _(t, c=c):
                    wat(t, sX, bX_g + 32)
                    for j in range(64):
                        if j >= 2:
                            wat(t, sV, bV_g + j - 1)
                        pt = pt0 if j % 2 == 0 else pt1
                        inc(t.transpose(pt[:, 0:128],
                                        ynchunk[:, j * 128:(j + 1) * 128],
                                        c_idb[:, :]), sT, 1)

                @block.vector
                def _(v, c=c):
                    wat(v, sS, bS_g + 2)
                    for j in range(64):
                        wat(v, sT, bT_g + j + 1)
                        pt = pt0 if j % 2 == 0 else pt1
                        inc(v.scalar_tensor_tensor(
                            ygat[:, j * T + c * 128:j * T + (c + 1) * 128],
                            pt[:, 0:128], c_normw[:, j:j + 1],
                            sgchunk[:, j * 128:(j + 1) * 128],
                            ALU.mult, ALU.mult), sV, 1)
                barrier()


        def s_m7():
            bE_o = sE.v
            bSbv2 = [s.v for s in sSb]
            bLbv2 = [s.v for s in sLb]
            o_load = [bLbv2[i % 3] + 16 * (i // 3 + 1) for i in range(KD)]

            @block.sync
            def _(sy):
                fence(sy)
                for i in range(KD):
                    if i >= 3:
                        wat(sy, sSb[i % 3],
                            bSbv2[i % 3] + 16 * ((i - 3) // 3 + 1))
                    inc(sy.dma_start(out=xtileM[i % 3][:, :],
                                     in_=x1_s.ap()[i * 128:(i + 1) * 128, :]),
                        sLb[i % 3], 16)
                    if i >= 1:
                        wat(sy, sE, bE_o + i)
                        inc(sy.dma_start(
                            out=x2_s.ap()[(i - 1) * 128:i * 128, :],
                            in_=xtileM[(i - 1) % 3][:, :]), sSb[(i - 1) % 3],
                            16)
                wat(sy, sE, bE_o + KD)
                inc(sy.dma_start(
                    out=x2_s.ap()[(KD - 1) * 128:KD * 128, :],
                    in_=xtileM[(KD - 1) % 3][:, :]), sSb[(KD - 1) % 3], 16)

            def ev_outp(i, ps, mval):
                @block.vector
                def _(v):
                    wat(v, mval[0], mval[1])
                    wat(v, sLb[i % 3], o_load[i])
                    inc(v.tensor_add(xtileM[i % 3][:, :], ps[:, :],
                                     xtileM[i % 3][:, :]), sE, 1)
            mm_stream(woutp, KD, KI, ygat, ev_outp)
            barrier()


        def s_ffn2():
            emit_ffn(x2_s, out_xT, 0.5, True)

        stages = [s_ffn1, s_mln, s_inproj, s_ag1, s_m2a, s_dexp,
                  s_conv, s_ssd, s_m4, s_m5, s_m6, s_m7, s_ffn2]
        for _i, _st in enumerate(stages):
            if _i < _UPTO:
                _st()

        if _DEBUG_TAP:
            tapb = at("tapb", 0, (128, INTER), BF16)
            tapf = [at(f"tapf{i}", 16 + 32 * i, (128, INTER), F32)
                    for i in range(2)]
            for name in _DEBUG_TAP:
                h, shp, dt = scr_handles[name]
                outt = nc.dram_tensor(name + "_tap", shp, F32,
                                      kind="ExternalOutput")
                nrow = shp[0] // 128
                cols = shp[1]
                bV_z = sV.v
                bLbv_z = [s.v for s in sLb]
                bSbv_z = [s.v for s in sSb]
                lv_z = []

                @block.sync
                def _(sy, h=h, nrow=nrow, cols=cols, dt=dt, outt=outt,
                      bV_z=bV_z, bSbv_z=bSbv_z):
                    fence(sy)
                    for r in range(nrow):
                        if r >= 2:
                            wat(sy, sSb[r % 2],
                                bSbv_z[r % 2] + 16 * ((r - 2) // 2 + 1))
                        if dt == BF16:
                            pass
                        else:
                            inc(sy.dma_start(
                                out=tapf[r % 2][:, 0:cols],
                                in_=h.ap()[r * 128:(r + 1) * 128, :]),
                                sLb[r % 2], 16)
                            lv_z.append(sLb[r % 2].v)
                            wat(sy, sLb[r % 2], lv_z[r])
                        if dt == BF16:
                            wat(sy, sV, bV_z + r + 1)
                        inc(sy.dma_start(
                            out=outt.ap()[r * 128:(r + 1) * 128, :],
                            in_=tapf[r % 2][:, 0:cols]), sSb[r % 2], 16)
                    wat(sy, sSb[0])
                    wat(sy, sSb[1])

                if dt == BF16:
                    bG_z = sG.v
                    glv = []

                    @block.gpsimd
                    def _(g, h=h, nrow=nrow, cols=cols, bV_z=bV_z):
                        for r in range(nrow):
                            if r >= 1:
                                wat(g, sV, bV_z + r)
                            inc(g.dma_start(
                                out=tapb[:, 0:cols],
                                in_=h.ap()[r * 128:(r + 1) * 128, :]),
                                sLa[0], 16)
                            glv.append(sLa[0].v)

                    @block.vector
                    def _(v, nrow=nrow, cols=cols, bSbv_z=bSbv_z):
                        for r in range(nrow):
                            wat(v, sLa[0], glv[r])
                            if r >= 2:
                                wat(v, sSb[r % 2],
                                    bSbv_z[r % 2] + 16 * ((r - 2) // 2 + 1))
                            inc(v.tensor_copy(tapf[r % 2][:, 0:cols],
                                              tapb[:, 0:cols]), sV, 1)
                barrier()
    return nc



def _img(w, n_out, n_k):
    K = n_k * 128
    return np.ascontiguousarray(
        w.reshape(n_k, 128, n_out, 128).transpose(2, 1, 0, 3)
        .reshape(n_out, 128, K).astype(np.float32))


def stage_inputs(L0_attn_hidden_states, L0_attn_in_proj_w, L0_attn_conv_w,
                 L0_attn_conv_b, L0_attn_dt_bias, L0_attn_A_log, L0_attn_D,
                 L0_attn_norm_w, L0_attn_out_proj_w, L0_ffn_gate_w,
                 L0_ffn_up_w, L0_ffn_down_w, L0_ffn_fc1_w, L0_ffn_fc2_w,
                 L0_ffn_expert_w, L0_ffn_ln_gamma, L0_ffn_ln_beta):
    x = np.asarray(L0_attn_hidden_states, dtype=np.float32)

    wgate_h = np.ascontiguousarray(
        np.asarray(L0_ffn_gate_w, np.float32).reshape(KD, 128, 8)
        .transpose(1, 0, 2).reshape(128, KD * 8))
    wfc1_h = _img(np.asarray(L0_ffn_fc1_w, np.float32), KL, KD)
    wexp_h = _img(np.asarray(L0_ffn_expert_w, np.float32), KL, KL)
    wfc2_h = _img(np.asarray(L0_ffn_fc2_w, np.float32), KD, KL)
    wup_h = _img(np.asarray(L0_ffn_up_w, np.float32), KI, KD)
    wdown_h = _img(np.asarray(L0_ffn_down_w, np.float32), KD, KI)
    inprojT = np.ascontiguousarray(
        np.asarray(L0_attn_in_proj_w, np.float32).T)
    winp_full = _img(inprojT, MP, KD)
    perm = [144] + list(range(64, 144)) + list(range(0, 64))
    winp_h = np.ascontiguousarray(winp_full[perm])
    outprojT = np.ascontiguousarray(
        np.asarray(L0_attn_out_proj_w, np.float32).T)
    woutp_h = _img(outprojT, KD, KI)

    convw_h = np.ascontiguousarray(
        np.asarray(L0_attn_conv_w, np.float32)[:, 0, :]
        .reshape(MC, 128, 4).transpose(1, 0, 2).reshape(128, MC * 4))
    convb_h = np.ascontiguousarray(
        np.asarray(L0_attn_conv_b, np.float32).reshape(MC, 128).T)
    dtb_h = np.asarray(L0_attn_dt_bias, np.float32).reshape(128, 1)
    alog_h = np.asarray(L0_attn_A_log, np.float32).reshape(128, 1)
    dcol_h = np.asarray(L0_attn_D, np.float32).reshape(128, 1)
    normw_h = np.ascontiguousarray(
        np.asarray(L0_attn_norm_w, np.float32).reshape(KI, 128).T)
    gam_h = np.ascontiguousarray(
        np.asarray(L0_ffn_ln_gamma, np.float32).reshape(KD, 128).T)
    bet_h = np.ascontiguousarray(
        np.asarray(L0_ffn_ln_beta, np.float32).reshape(KD, 128).T)

    shared = dict(wgate=wgate_h, wfc1=wfc1_h, wexp=wexp_h, wfc2=wfc2_h,
                  wup=wup_h, wdown=wdown_h, winp=winp_h, woutp=woutp_h,
                  convw=convw_h, convb=convb_h, dtbias=dtb_h, alog=alog_h,
                  dcol=dcol_h, normw=normw_h, gam=gam_h, bet=bet_h)

    in_maps = []
    for k in range(NCORE):
        b, q = divmod(k, 4)
        xTk = np.ascontiguousarray(x[b].T[:, q * T:(q + 1) * T])
        nm = np.zeros((128, 4), np.float32)
        hm = np.zeros((128, 4), np.float32)
        if q > 0:
            nm[:, q - 1] = 1.0
            hm[:, :q] = 1.0
        m = dict(shared)
        m["xT"] = xTk
        m["nmask"] = nm
        m["hmask"] = hm
        in_maps.append(m)
    return in_maps


def kernel(**inputs):
    in_maps = stage_inputs(**inputs)
    nc = build_program()
    res = run_bass_kernel_spmd(nc, in_maps, core_ids=list(range(NCORE)))
    out = np.empty((2, 4 * T, D), np.float32)
    for k in range(NCORE):
        b, q = divmod(k, 4)
        out[b, q * T:(q + 1) * T, :] = res.results[k]["out_xT"].T
    return out
